# version 20
# baseline (speedup 1.0000x reference)
import numpy as np
import ml_dtypes
import concourse.bass as bass
import concourse.mybir as mybir
from concourse.bass_utils import run_bass_kernel_spmd

F32 = mybir.dt.float32
BF16 = mybir.dt.bfloat16
U32 = mybir.dt.uint32
AF = mybir.ActivationFunctionType
ALU = mybir.AluOpType
AX = mybir.AxisListType
BFNP = ml_dtypes.bfloat16

T = 2048
D = 1024
S = 512
E = 8
NT = 16
DC = 8
SC = 4
MB = 32
CAP = 384
NST = 3

ENGS = ("t", "v", "a", "s", "g")


class Prog:
    def __init__(self, sems):
        self.sems = sems
        self.q = {e: [] for e in ENGS}
        self.cnt = {k: 0 for k in sems}
        self.lastwait = {}

    def wait(self, eng, src, val):
        if val is None or val <= 0:
            return
        assert val <= self.cnt[src], f"wait on {src}={val} but only {self.cnt[src]} emitted"
        k = (eng, src)
        if self.lastwait.get(k, 0) >= val:
            return
        self.lastwait[k] = val
        sem = self.sems[src]
        self.q[eng].append(lambda e, sem=sem, v=val: e.wait_ge(sem, v))

    def op(self, eng, fn, inc=True, stream=None):
        if eng in ("v", "a"):
            self.wait(eng, eng, self.cnt[eng])
        if inc:
            key = stream or eng
            self.cnt[key] += 1
            v = self.cnt[key]
            sem = self.sems[key]
            self.q[eng].append(lambda e, fn=fn, sem=sem: fn(e).then_inc(sem, 1))
            return v
        self.q[eng].append(lambda e, fn=fn: fn(e))
        return None

    def dma(self, eng, stream, dst, src, **kw):
        self.cnt[stream] += 16
        v = self.cnt[stream]
        sem = self.sems[stream]
        self.q[eng].append(
            lambda e, dst=dst, src=src, kw=kw, sem=sem: e.dma_start(dst, src, **kw).then_inc(sem, 16))
        return v


def build():
    nc = bass.Bass("TRN2", target_bir_lowering=False)

    dram = {}
    for nm, sh in [("x", [T, D]), ("w1", [128, E * MB * DC * 128]),
                   ("w2", [128, E * MB * DC * 128]), ("bmat", [128, DC * SC * 128]),
                   ("cmat", [128, SC * D]), ("wg", [128, 64]), ("bgv", [8, 1]),
                   ("dtv", [128, 4]), ("av", [128, 4]), ("dvec", [128, 8]),
                   ("nw", [128, 8]), ("b1v", [128, 256]), ("b2v", [128, 64]),
                   ("id32", [128, 128]), ("i384", [128, CAP]), ("iP16", [16, 128]),
                   ("iPst", [128, NST]), ("i8", [128, 8]), ("onec", [1, 128])]:
        dram[nm] = nc.dram_tensor(nm, sh, F32, kind="ExternalInput").ap()
    dram["out"] = nc.dram_tensor("out", [T, D], F32, kind="ExternalOutput").ap()
    dram["rowb"] = nc.dram_tensor("rowb_sc", [16, 128], F32).ap()
    dram["rowb2"] = nc.dram_tensor("rowb2_sc", [16, 128], F32).ap()
    dram["rowb3"] = nc.dram_tensor("rowb3_sc", [8, 128], F32).ap()

    SB_BASE = 16512
    LIMIT = 229376
    cur = [SB_BASE]
    A = {}
    OFF = {}

    def at(name, shape, dt, size=None):
        if size is None:
            nb = {F32: 4, BF16: 2, U32: 4}[dt]
            size = int(np.prod(shape[1:])) * nb
        off = cur[0]
        cur[0] += (size + 31) & ~31
        assert cur[0] <= LIMIT, f"SBUF overflow at {name}: {cur[0]}"
        A[name] = nc.alloc_sbuf_tensor_at(name, shape, dt, offset=off)
        OFF[name] = off
        return off

    def over(name, shape, dt, off):
        A[name] = nc.alloc_sbuf_tensor_at(name, shape, dt, offset=off)

    for nm, sh, dt in [
        ("b1v", [128, 256], F32), ("b2v", [128, 64], F32), ("dvec", [128, 8], F32),
        ("nw", [128, 8], F32), ("dtv", [128, 4], F32), ("av", [128, 4], F32),
        ("cst", [128, 2], F32), ("bgv", [8, 1], F32), ("wg", [128, 64], F32),
        ("wgh", [128, 64], BF16), ("wgl", [128, 64], BF16), ("dts", [128, 4], F32),
        ("a_s", [128, 4], F32), ("small", [128, 16], F32), ("mv", [128, 32], F32),
        ("bnbuf", [128, 12], F32), ("statw", [128, 64], F32), ("id32", [128, 128], F32),
        ("iota8", [128, 8], F32), ("i384", [128, CAP], F32), ("iP16", [16, 128], F32),
        ("iPst", [128, NST], F32), ("onehot", [16, 256], F32), ("dvecT", [8, 128], F32),
        ("rstdT", [16, 128], F32), ("p16T", [16, 128], F32), ("mx16", [128, 16], F32),
        ("z16", [128, 16], F32), ("p16f", [128, 16], F32), ("ixf16", [128, 16], F32),
        ("lgall", [128, 128], F32), ("subal", [128, 128], F32), ("maskal", [128, 128], F32),
        ("idxual", [128, 128], U32), ("sposc", [128, 128], F32),
        ("lo2", [128, 256], BF16), ("ybf", [128, 2 * CAP], F32),
        ("onec", [1, 128], F32),
    ]:
        at(nm, sh, dt)
    at("spos0", [8, T], F32)
    at("prow", [128, T], BF16)
    at("xn_tok", [128, NT * D], BF16)
    at("out_acc", [128, NT * D], F32)
    WST = at("wstage", [128, 4 * 1024], F32)
    W1R = at("w1r", [128, 4 * 1024], BF16)
    W2R = at("w2r", [128, 4 * 1024], BF16)
    HSB = at("h_sb", [128, MB * CAP], BF16)
    PRG = at("pring", [128, 2 * CAP], BF16)
    PTO = at("pt", [128, NST * T], BF16)
    XGS = at("xgs", [128, 2048], F32)
    YTO = at("yt", [128, NST * 1024], BF16, size=8192)
    assert cur[0] <= LIMIT, cur[0]

    over("Xg", [128, DC * CAP], BF16, XGS)
    over("sbc", [128, T], F32, XGS)
    over("bmbf", [128, 4096], BF16, WST)
    over("Dbc", [128, D], F32, WST + 8192)
    over("scin", [128, T], F32, WST + 8192)
    over("xnt32_t0", [128, D], F32, WST + 12288)
    over("lg", [8, T], F32, W1R)
    over("xscan0", [128, T], BF16, W1R)
    over("xscan1", [128, T], BF16, W1R + 4096)
    over("mask01", [8, T], F32, W2R)
    over("xscan2", [128, T], BF16, W2R)
    over("xscan3", [128, T], BF16, W2R + 4096)
    over("xnt16", [128, DC * T], BF16, HSB)
    over("xnt32_t1", [128, D], F32, PTO + 6656)
    over("rstd_b", [128, T], F32, XGS)
    over("xn32t0", [128, D], F32, XGS)
    over("xn32t1", [128, D], F32, XGS + 4096)
    over("incl", [8, T], F32, WST + 8192)
    over("cmbf", [128, 4096], BF16, YTO)
    over("lo2f", [128, D], BF16, OFF["ybf"])
    over("w1x", [128, 4 * 1024], BF16, HSB + 16384)

    import contextlib
    with contextlib.ExitStack() as stk:
        pb = [stk.enter_context(nc.psum_tensor(f"pb{i}", [128, 512], F32)) for i in range(8)]
        sem_names = ["t", "v", "a", "sc", "gc", "go", "q0", "q1", "q2"]
        sems = {n: stk.enter_context(nc.semaphore(f"sem_{n}")) for n in sem_names}
        p = Prog(sems)
        emit(p, A, pb, dram)

        with nc.Block() as blk:
            @blk.tensor
            def _(eng):
                for fn in p.q["t"]:
                    fn(eng)

            @blk.vector
            def _(eng):
                for fn in p.q["v"]:
                    fn(eng)

            @blk.scalar
            def _(eng):
                for fn in p.q["a"]:
                    fn(eng)

            @blk.sync
            def _(eng):
                for fn in p.q["s"]:
                    fn(eng)

            @blk.gpsimd
            def _(eng):
                for fn in p.q["g"]:
                    fn(eng)
    return nc


def emit(p, A, pb, dram):
    g = lambda n: A[n]
    out_acc, xn_tok, xnt16 = g("out_acc"), g("xn_tok"), g("xnt16")
    statw, mv, bnbuf, cst, small = g("statw"), g("mv"), g("bnbuf"), g("cst"), g("small")
    id32, onehot, iP16 = g("id32"), g("onehot"), g("iP16")

    for nm_s, nm_d in [("b1v", "b1v"), ("b2v", "b2v"), ("dvec", "dvec"), ("nw", "nw"),
                       ("dtv", "dtv"), ("av", "av"), ("bgv", "bgv"), ("wg", "wg"),
                       ("id32", "id32"), ("i384", "i384"), ("iP16", "iP16"),
                       ("iPst", "iPst"), ("iota8", "i8"), ("onec", "onec")]:
        p.dma("s", "sc", g(nm_s)[:, :], dram[nm_d])
    c_consts = p.cnt["sc"]

    p.op("v", lambda e: e.memset(cst[:, 0:1], 1e-5))
    p.op("v", lambda e: e.memset(cst[:, 1:2], 1e-6))
    v_one = p.op("v", lambda e: e.memset(small[:, 0:1], 1.0))

    p.wait("g", "sc", c_consts)
    swx_bm = p.dma("g", "gc", g("bmbf")[:, :], dram["bmat"])

    p.wait("t", "sc", c_consts)
    t_dT = p.op("t", lambda e: e.transpose(pb[0][0:8, 0:128], g("dvec")[:, :], id32[:, :]))
    p.wait("a", "t", t_dT)
    a_dT = p.op("a", lambda e: e.copy(g("dvecT")[:, :], pb[0][0:8, 0:128]))
    oh_free_t = {0: None, 1: None}
    p.wait("s", "a", a_dT)
    bd1 = p.dma("s", "sc", dram["rowb3"], g("dvecT")[:, :])
    p.wait("s", "sc", bd1)
    bd2 = p.dma("s", "sc", g("Dbc")[0:1, :],
                dram["rowb3"].rearrange("(o a) b -> o (a b)", o=1))
    v_dbc_all = 0
    for hh in range(2):
        bank = hh
        p.wait("t", "sc", bd2)
        tm = p.op("t", lambda e, bank=bank, hh=hh: e.matmul(
            pb[bank][:, 0:512], g("onec")[:, :], g("Dbc")[0:1, hh * 512:(hh + 1) * 512],
            start=True, stop=True))
        p.wait("v", "t", tm)
        v_dbc_all = p.op("v", lambda e, bank=bank, hh=hh: e.tensor_copy(
            g("Dbc")[:, hh * 512:(hh + 1) * 512], pb[bank][:, 0:512]))

    v_wgh = p.op("v", lambda e: e.tensor_copy(g("wgh")[:, :], g("wg")[:, :]))
    v_wgl = p.op("v", lambda e: e.tensor_tensor(g("wgl")[:, :], g("wg")[:, :],
                                                g("wgh")[:, :], ALU.subtract))

    xn32t = [g("xn32t0"), g("xn32t1")]
    xnt32t = [g("xnt32_t0"), g("xnt32_t1")]
    lo2 = g("lo2")
    x_d = dram["x"]
    v_xn, v_tok, a_ev, trv_last, vlo_last = {}, {}, {}, {}, {}
    gate_grp = {}
    a_lg = {}
    tr_i = 0
    for i in range(NT):
        buf = i % 2
        xld = p.dma("s", "sc", out_acc[:, i * D:(i + 1) * D], x_d[i * 128:(i + 1) * 128, :])
        p.wait("v", "sc", xld)
        p.op("v", lambda e, i=i: e.bn_stats(bnbuf[:, 0:6], out_acc[:, i * D:i * D + 512]))
        p.op("v", lambda e, i=i: e.bn_stats(bnbuf[:, 6:12], out_acc[:, i * D + 512:(i + 1) * D]))
        v3 = p.op("v", lambda e, i=i: e.bn_aggr(mv[:, 2 * i:2 * i + 2], bnbuf[:, 0:12]))
        p.wait("a", "v", v3)
        a1 = p.op("a", lambda e, i=i: e.activation(statw[:, 49:50],
                                                   mv[:, 2 * i + 1:2 * i + 2], AF.Ln,
                                                   bias=cst[:, 0:1]))
        a2 = p.op("a", lambda e, i=i: e.activation(statw[:, i:i + 1], statw[:, 49:50],
                                                   AF.Exp, scale=-0.5))
        p.wait("v", "a", a2)
        v5 = p.op("v", lambda e, i=i: e.scalar_tensor_tensor(
            statw[:, 16 + i:17 + i], mv[:, 2 * i:2 * i + 1], -1.0, statw[:, i:i + 1],
            ALU.mult, ALU.mult))
        if i >= 2:
            p.wait("a", "t", trv_last[i - 2])
            p.wait("a", "v", v_tok[i - 2])
        p.wait("a", "v", v5)
        v_xn[i] = p.op("a", lambda e, i=i, b=buf: e.activation(
            xn32t[b][:, :], out_acc[:, i * D:(i + 1) * D], AF.Identity,
            scale=statw[:, i:i + 1], bias=statw[:, 16 + i:17 + i]))
        p.wait("v", "a", v_xn[i])
        v_tok[i] = p.op("v", lambda e, i=i, b=buf: e.tensor_copy(
            xn_tok[:, i * D:(i + 1) * D], xn32t[b][:, :]))
        p.wait("v", "v", v_dbc_all)
        p.op("v", lambda e, i=i: e.tensor_tensor(
            out_acc[:, i * D:(i + 1) * D], out_acc[:, i * D:(i + 1) * D],
            g("Dbc")[:, :], ALU.add))
        p.wait("t", "a", v_xn[i])
        tv = None
        for half in range(2):
            bank = 6 + half
            if i >= 1:
                p.wait("t", "a", a_ev[(i - 1, half)])
            for k4 in range(4):
                dc = half * 4 + k4
                tv = p.op("t", lambda e, bank=bank, dc=dc, k4=k4, b=buf: e.transpose(
                    pb[bank][:, k4 * 128:(k4 + 1) * 128],
                    xn32t[b][:, dc * 128:(dc + 1) * 128], id32[:, :]))
            p.wait("a", "t", tv)
            if i >= 2 and half == 0:
                p.wait("a", "v", vlo_last[i - 2])
            a_ev[(i, half)] = p.op("a", lambda e, bank=bank, half=half, b=buf: e.copy(
                xnt32t[b][:, half * 512:(half + 1) * 512], pb[bank][:, 0:512]))
        trv_last[i] = tv
        p.wait("v", "a", a_ev[(i, 1)])
        dst3 = xnt16.rearrange("p (dc t) -> p dc t", dc=DC)[:, :, i * 128:(i + 1) * 128]
        src3 = xnt32t[buf][:, :].rearrange("p (dc k) -> p dc k", dc=DC)
        vhi = p.op("v", lambda e, dst3=dst3, src3=src3: e.tensor_copy(dst3, src3))
        if i >= 1:
            p.wait("v", "t", gate_grp[(i - 1, DC - 1)])
        lo3 = g("lo2f")[:, :].rearrange("p (dc k) -> p dc k", dc=DC)
        vlo = p.op("v", lambda e, src3=src3, dst3=dst3, lo3=lo3: e.tensor_tensor(
            lo3, src3, dst3, ALU.subtract))
        vlo_last[i] = vlo
        bank = 2 + i // 4
        col = (i % 4) * 128
        first = True
        for dc in range(DC):
            p.wait("t", "v", vhi)
            p.wait("t", "v", vlo)
            p.wait("t", "v", v_wgl)
            last = None
            for pi, (wgt, rhs) in enumerate([
                    (g("wgh"), xnt16[:, dc * T + i * 128:dc * T + (i + 1) * 128]),
                    (g("wgh"), g("lo2f")[:, dc * 128:(dc + 1) * 128]),
                    (g("wgl"), xnt16[:, dc * T + i * 128:dc * T + (i + 1) * 128])]):
                st_ = first
                first = False
                sp_ = (dc == DC - 1 and pi == 2)
                r = p.op("t", lambda e, bank=bank, col=col, wgt=wgt, rhs=rhs, dc=dc,
                         st_=st_, sp_=sp_: e.matmul(
                    pb[bank][0:8, col:col + 128], wgt[:, dc * 8:(dc + 1) * 8], rhs,
                    start=st_, stop=sp_), inc=(pi == 2))
                if r is not None:
                    last = r
            gate_grp[(i, dc)] = last
        if i % 4 == 3:
            p.wait("a", "t", gate_grp[(i, DC - 1)])
            a_lg[i // 4] = p.op("a", lambda e, bank=bank, q=i // 4: e.activation(
                g("lg")[0:8, q * 512:(q + 1) * 512], pb[bank][0:8, 0:512], AF.Identity,
                bias=g("bgv")[:, 0:1]))
    gate_last_t = gate_grp[(NT - 1, DC - 1)]

    a1b = p.op("a", lambda e: e.activation(statw[:, 48:64], mv[:, 1:32:2], AF.Ln, bias=cst[:, 0:1]))
    a2b = p.op("a", lambda e: e.activation(statw[:, 48:64], statw[:, 48:64], AF.Exp, scale=-1.0))
    p.wait("v", "a", a2b)
    v_q = p.op("v", lambda e: e.tensor_tensor(statw[:, 32:48], mv[:, 1:32:2],
                                              statw[:, 48:64], ALU.mult))
    p.wait("a", "v", v_q)
    p.op("a", lambda e: e.activation(statw[:, 48:64], statw[:, 32:48], AF.Ln, bias=cst[:, 1:2]))
    a4b = p.op("a", lambda e: e.activation(statw[:, 32:48], statw[:, 48:64], AF.Exp, scale=-0.5))

    p.wait("t", "a", a4b)
    p.wait("t", "a", a_ev[(NT - 1, 0)])
    tvr = p.op("t", lambda e: e.transpose(pb[6][0:16, 0:128], statw[:, 32:48], id32[:, :]))
    p.wait("a", "t", tvr)
    a_rT = p.op("a", lambda e: e.copy(g("rstdT")[:, :], pb[6][0:16, 0:128]))
    p.wait("s", "a", a_rT)
    br1 = p.dma("s", "sc", dram["rowb"], g("rstdT")[:, :])
    p.wait("s", "sc", br1)
    p.wait("s", "v", vlo_last[NT - 1])
    br2 = p.dma("s", "sc", g("rstd_b")[0:1, :],
                dram["rowb"].rearrange("(o a) b -> o (a b)", o=1))
    rstd_done = {}
    for tc4 in range(4):
        bank = 4 + tc4 % 2
        p.wait("t", "sc", br2)
        p.wait("t", "a", a_lg[2] if tc4 % 2 == 0 else a_lg[3])
        if tc4 >= 2:
            p.wait("t", "v", rstd_done[tc4 - 2])
        tm = p.op("t", lambda e, bank=bank, tc4=tc4: e.matmul(
            pb[bank][:, 0:512], g("onec")[:, :],
            g("rstd_b")[0:1, tc4 * 512:(tc4 + 1) * 512], start=True, stop=True))
        p.wait("v", "t", tm)
        rstd_done[tc4] = p.op("v", lambda e, bank=bank, tc4=tc4: e.tensor_copy(
            g("rstd_b")[:, tc4 * 512:(tc4 + 1) * 512], pb[bank][:, 0:512]))

    tm1_all = None
    p.wait("t", "a", a_rT)
    for i in range(NT):
        p.wait("t", "a", a_lg[i // 4])
        tm1_all = p.op("t", lambda e, i=i: e.transpose(
            pb[6][:, i * 8:(i + 1) * 8], g("lg")[0:8, i * 128:(i + 1) * 128],
            id32[0:8, 0:8]))
    lg3 = lambda ap: ap.rearrange("p (s j) -> p s j", s=16)
    mxB = g("mx16")[:, :].unsqueeze(2).broadcast_to([128, 16, 8])
    p.wait("v", "t", tm1_all)
    v_lgall = p.op("v", lambda e: e.tensor_copy(g("lgall")[:, :], pb[6][:, 0:128]))
    p.op("v", lambda e: e.tensor_reduce(g("mx16")[:, :], lg3(g("lgall")[:, :]), axis=AX.X,
                                        op=ALU.max))
    v_sub = p.op("v", lambda e: e.tensor_tensor(lg3(g("subal")[:, :]), lg3(g("lgall")[:, :]),
                                                mxB, ALU.subtract))
    p.op("v", lambda e: e.tensor_tensor(lg3(g("maskal")[:, :]), mxB, mxB, ALU.max))
    for i in range(NT):
        p.op("v", lambda e, i=i: e.max_index(
            g("idxual")[:, i * 8:(i + 1) * 8], g("maskal")[:, i * 8:(i + 1) * 8],
            g("lgall")[:, i * 8:(i + 1) * 8]))
    v_ixf = p.op("v", lambda e: e.tensor_copy(g("ixf16")[:, :], g("idxual")[:, 0:128:8]))
    p.wait("a", "v", v_ixf)
    aex = p.op("a", lambda e: e.activation(g("lgall")[:, :], g("subal")[:, :], AF.Exp))
    p.wait("v", "a", aex)
    p.op("v", lambda e: e.tensor_reduce(g("z16")[:, :], lg3(g("lgall")[:, :]), axis=AX.X,
                                        op=ALU.add))
    p.op("v", lambda e: e.reciprocal(g("p16f")[:, :], g("z16")[:, :]))
    vmk = None
    for i in range(NT):
        vmk = p.op("v", lambda e, i=i: e.tensor_scalar(
            g("maskal")[:, i * 8:(i + 1) * 8], g("iota8")[:, :], g("ixf16")[:, i:i + 1],
            None, ALU.is_equal, ALU.bypass))

    p.wait("t", "v", rstd_done[2])
    p.wait("t", "v", rstd_done[3])
    p.wait("t", "v", vmk)
    tpT = p.op("t", lambda e: e.transpose(pb[4][0:16, 0:128], g("p16f")[:, :], id32[:, :]))
    p.wait("a", "t", tpT)
    a_pT = p.op("a", lambda e: e.copy(g("p16T")[:, :], pb[4][0:16, 0:128]))
    p.wait("s", "a", a_pT)
    bp1 = p.dma("s", "sc", dram["rowb2"], g("p16T")[:, :])
    p.wait("s", "sc", bp1)
    bp2 = p.dma("s", "sc", g("scin")[0:1, :],
                dram["rowb2"].rearrange("(o a) b -> o (a b)", o=1))
    prow_done = {}
    for tc4 in range(4):
        bank = 4 + tc4 % 2
        p.wait("t", "sc", bp2)
        if tc4 >= 2:
            p.wait("t", "v", prow_done[tc4 - 2])
        else:
            p.wait("t", "v", rstd_done[2 + tc4])
        tm = p.op("t", lambda e, bank=bank, tc4=tc4: e.matmul(
            pb[bank][:, 0:512], g("onec")[:, :],
            g("scin")[0:1, tc4 * 512:(tc4 + 1) * 512], start=True, stop=True))
        p.wait("v", "t", tm)
        prow_done[tc4] = p.op("v", lambda e, bank=bank, tc4=tc4: e.tensor_copy(
            g("prow")[:, tc4 * 512:(tc4 + 1) * 512], pb[bank][:, 0:512]))

    v_m01 = None
    for gq in range(4):
        p.wait("t", "v", vmk)
        if gq >= 1:
            p.wait("t", "v", v_m01)
        tt2 = None
        for k in range(4):
            i = gq * 4 + k
            tt2 = p.op("t", lambda e, i=i, k=k: e.transpose(
                pb[7][0:8, k * 128:(k + 1) * 128], g("maskal")[:, i * 8:(i + 1) * 8],
                id32[:, :]))
        p.wait("v", "t", tt2)
        v_m01 = p.op("v", lambda e, gq=gq: e.tensor_copy(
            g("mask01")[0:8, gq * 512:(gq + 1) * 512], pb[7][0:8, 0:512]))

    p.wait("v", "v", v_one)
    p.op("v", lambda e: e.tensor_tensor_scan(
        g("incl")[0:8, :], small[0:8, 0:1].broadcast_to([8, T]), g("mask01")[0:8, :],
        0.0, ALU.mult, ALU.add))
    p.op("v", lambda e: e.tensor_tensor(g("spos0")[0:8, :], g("incl")[0:8, :],
                                        g("mask01")[0:8, :], ALU.mult))
    v_spos = p.op("v", lambda e: e.tensor_scalar(
        g("spos0")[0:8, :], g("spos0")[0:8, :], 1.0, None, ALU.subtract, ALU.bypass))

    a_sposc = {}
    p.wait("t", "v", v_lgall)
    p.wait("t", "v", v_spos)
    for i in range(NT):
        col = (i % 8) * 8
        if i >= 8:
            p.wait("t", "a", a_sposc[i - 8])
        ts = p.op("t", lambda e, i=i, col=col: e.transpose(
            pb[6][:, col:col + 8], g("spos0")[0:8, i * 128:(i + 1) * 128],
            id32[0:8, 0:8]))
        p.wait("a", "t", ts)
        a_sposc[i] = p.op("a", lambda e, i=i, col=col: e.copy(
            g("sposc")[:, i * 8:(i + 1) * 8], pb[6][:, col:col + 8]))

    v_bm = {}
    for ic in range(DC):
        p.wait("v", "gc", swx_bm)
        v_bm[ic] = p.op("v", lambda e, ic=ic: e.tensor_scalar(
            g("bmbf")[:, ic * 512:(ic + 1) * 512], g("bmbf")[:, ic * 512:(ic + 1) * 512],
            g("nw")[:, ic:ic + 1], None, ALU.mult, ALU.bypass))
    a_dts = p.op("a", lambda e: e.activation(g("dts")[:, :], g("dtv")[:, :], AF.Sigmoid))
    a_ab = p.op("a", lambda e: e.activation(small[:, 4:8], g("av")[:, :], AF.Abs))
    p.wait("v", "a", a_ab)
    v_pr = p.op("v", lambda e: e.tensor_tensor(small[:, 8:12], g("dts")[:, :],
                                               small[:, 4:8], ALU.mult))
    p.wait("a", "v", v_pr)
    a_a = p.op("a", lambda e: e.activation(g("a_s")[:, :], small[:, 8:12], AF.Exp, scale=-1.0))

    p.wait("g", "t", gate_last_t)
    swx_cm = p.dma("g", "gc", g("cmbf")[:, :], dram["cmat"])

    xscan = [g("xscan0"), g("xscan1"), g("xscan2"), g("xscan3")]
    v_scan, v_si = {}, {}
    u_last = 0
    for sc in range(SC):
        p.wait("v", "a", a_a)
        for tc in range(4):
            bank = tc
            tu = None
            for ic in range(DC):
                p.wait("t", "v", v_bm[ic])
                p.wait("t", "v", v_dbc_all)
                p.wait("t", "a", a_lg[0])
                p.wait("t", "a", a_lg[1])
                if sc > 0:
                    p.wait("t", "v", v_si[(sc - 1, tc)])
                r = p.op("t", lambda e, bank=bank, ic=ic, sc=sc, tc=tc: e.matmul(
                    pb[bank][:, 0:512],
                    g("bmbf")[:, (ic * SC + sc) * 128:(ic * SC + sc) * 128 + 128],
                    xnt16[:, ic * T + tc * 512:ic * T + (tc + 1) * 512],
                    start=(ic == 0), stop=(ic == DC - 1)), inc=(ic == DC - 1))
                if r is not None:
                    tu = r
            u_last = tu
            p.wait("v", "t", tu)
            v_si[(sc, tc)] = p.op("v", lambda e, bank=bank, tc=tc, sc=sc: e.scalar_tensor_tensor(
                g("scin")[:, tc * 512:(tc + 1) * 512], pb[bank][:, 0:512],
                g("dts")[:, sc:sc + 1], g("rstd_b")[:, tc * 512:(tc + 1) * 512],
                ALU.mult, ALU.mult))
        p.wait("v", "t", tm1_all)
        v_scan[sc] = p.op("v", lambda e, sc=sc: e.tensor_tensor_scan(
            xscan[sc][:, :], g("a_s")[:, sc:sc + 1].broadcast_to([128, T]),
            g("scin")[:, :], 0.0, ALU.mult, ALU.add))

    c_add = {}
    c_last_t = 0
    for i in range(NT):
        for dh in range(2):
            idx = i * 2 + dh
            bank = 4 + idx % 4
            tcv = None
            for sc in range(SC):
                p.wait("t", "v", v_scan[sc])
                p.wait("t", "gc", swx_cm)
                if idx >= 4:
                    p.wait("t", "v", c_add[idx - 4])
                elif bank == 4:
                    p.wait("t", "v", prow_done[2])
                elif bank == 5:
                    p.wait("t", "v", prow_done[3])
                elif bank == 6:
                    p.wait("t", "a", a_sposc[NT - 1])
                else:
                    p.wait("t", "v", v_m01)
                r = p.op("t", lambda e, bank=bank, sc=sc, i=i, dh=dh: e.matmul(
                    pb[bank][:, 0:512],
                    xscan[sc][:, i * 128:(i + 1) * 128],
                    g("cmbf")[:, sc * 1024 + dh * 512:sc * 1024 + (dh + 1) * 512],
                    start=(sc == 0), stop=(sc == SC - 1)), inc=(sc == SC - 1))
                if r is not None:
                    tcv = r
            c_last_t = tcv
            p.wait("v", "t", tcv)
            c_add[idx] = p.op("v", lambda e, bank=bank, i=i, dh=dh: e.tensor_tensor(
                out_acc[:, i * D + dh * 512:i * D + (dh + 1) * 512],
                out_acc[:, i * D + dh * 512:i * D + (dh + 1) * 512],
                pb[bank][:, 0:512], ALU.add))
    c_add_last = c_add[NT * 2 - 1]

    qmap = ["q0", "q1", "q2"]
    qeng = {"q0": "s", "q1": "a", "q2": "g"}
    stage_cast = []
    stage_k = [0]
    stage_i = [0]
    ring_free = {1: {}, 2: {}}
    cast_val = {1: {}, 2: {}}

    def w_sched(e_, m, which, ylast_t=None):
        k = stage_k[0]
        stage_k[0] += 1
        q = qmap[k % 3]
        eng = qeng[q]
        srcd = dram["w1"] if which == 1 else dram["w2"]
        col = (e_ * MB + m) * DC * 128
        gidx = e_ * MB + m
        if which == 1 and m < 4:
            dst = g("w1x")[:, m * 1024:(m + 1) * 1024]
            war_t = ylast_t
        else:
            ring = g("w1r") if which == 1 else g("w2r")
            dst = ring[:, (m % 4) * 1024:(m % 4 + 1) * 1024]
            if which == 1:
                wg_ = gidx - 8 if m < 8 else gidx - 4
            else:
                wg_ = gidx - 4
            war_t = ring_free[which].get(wg_) if wg_ >= 0 else None
            if wg_ >= 0:
                assert wg_ in ring_free[which], (which, e_, m)
        if q == "q2":
            p.wait(eng, "t", war_t)
            p.wait(eng, "t", c_last_t)
            v = p.dma(eng, q, dst, srcd[:, col:col + DC * 128])
            cast_val[which][(e_, m)] = (q, v)
        else:
            ks = stage_i[0]
            stage_i[0] += 1
            slot = ks % 4
            if ks >= 4:
                p.wait(eng, "v", stage_cast[ks - 4])
            p.wait(eng, "t", c_last_t)
            v = p.dma(eng, q, g("wstage")[:, slot * 1024:(slot + 1) * 1024],
                      srcd[:, col:col + DC * 128])
            p.wait("v", q, v)
            p.wait("v", "t", war_t)
            cv = p.op("v", lambda e, dst=dst, slot=slot: e.tensor_copy(
                dst, g("wstage")[:, slot * 1024:(slot + 1) * 1024]))
            stage_cast.append(cv)
            cast_val[which][(e_, m)] = ("v", cv)

    gather_t_prev = {}
    scatter_add_last = c_add_last
    scatter_t_last = c_last_t
    pt_built = {}
    a_yt_store = {}
    relu = {}
    out_sent = 0

    for e_ in range(E):
        gather_t = {}
        for i in range(NT):
            pr = i % 2
            p.wait("v", "a", a_sposc[NT - 1])
            war = gather_t[i - 2] if i >= 2 else gather_t_prev.get(14 + i)
            p.wait("v", "t", war)
            if e_ == 0 and i < 2:
                p.wait("v", "t", u_last)
            pbuild = p.op("v", lambda e, i=i, pr=pr, e_=e_: e.tensor_scalar(
                g("pring")[:, pr * CAP:(pr + 1) * CAP], g("i384")[:, :],
                g("sposc")[:, i * 8 + e_:i * 8 + e_ + 1], None, ALU.is_equal, ALU.bypass))
            last = None
            for dc in range(DC):
                p.wait("t", "v", pbuild)
                p.wait("t", "v", v_tok[NT - 1])
                if i == 0:
                    p.wait("t", "v", scatter_add_last)
                r = p.op("t", lambda e, dc=dc, i=i, pr=pr: e.matmul(
                    pb[dc][:, 0:CAP], xn_tok[:, i * D + dc * 128:i * D + (dc + 1) * 128],
                    g("pring")[:, pr * CAP:(pr + 1) * CAP],
                    start=(i == 0), stop=(i == NT - 1)), inc=(dc == DC - 1))
                if r is not None:
                    last = r
            gather_t[i] = last
        gather_t_prev = gather_t
        a_xg = None
        for dc in range(DC):
            p.wait("a", "t", gather_t[NT - 1])
            if e_ > 0:
                p.wait("a", "v", pt_built[e_ - 1])
            a_xg = p.op("a", lambda e, dc=dc: e.activation(
                g("Xg")[:, dc * CAP:(dc + 1) * CAP], pb[dc][:, 0:CAP], AF.Identity))

        if e_ == 0:
            for m in range(4):
                w_sched(e_, m, 1)

        for m in range(MB):
            bank = m % 4
            cstr, cval = cast_val[1][(e_, m)]
            p.wait("t", cstr, cval)
            p.wait("t", "a", a_xg)
            if m >= 4:
                p.wait("t", "a", relu[(e_, m - 4)])
            th = None
            wsrc = g("w1x") if m < 4 else g("w1r")
            wbase = m * 1024 if m < 4 else (m % 4) * 1024
            for dc in range(DC):
                r = p.op("t", lambda e, bank=bank, m=m, dc=dc, wsrc=wsrc, wbase=wbase: e.matmul(
                    pb[bank][:, 0:CAP],
                    wsrc[:, wbase + dc * 128:wbase + (dc + 1) * 128],
                    g("Xg")[:, dc * CAP:(dc + 1) * CAP],
                    start=(dc == 0), stop=(dc == DC - 1)), inc=(dc == DC - 1))
                if r is not None:
                    th = r
            ring_free[1][e_ * MB + m] = th
            p.wait("a", "t", th)
            relu[(e_, m)] = p.op("a", lambda e, bank=bank, m=m, e_=e_: e.activation(
                g("h_sb")[:, m * CAP:(m + 1) * CAP], pb[bank][:, 0:CAP], AF.Relu,
                bias=g("b1v")[:, e_ * MB + m:e_ * MB + m + 1]))
            if m + 4 < MB:
                w_sched(e_, m + 4, 1)
        h_last_t = ring_free[1][e_ * MB + MB - 1]

        sbc_done = {}
        for tc in range(4):
            ohb = tc % 2
            p.wait("v", "t", oh_free_t[ohb])
            p.wait("v", "t", h_last_t)
            voh = p.op("v", lambda e, e_=e_, ohb=ohb: e.tensor_scalar(
                onehot[0:8, ohb * 128:(ohb + 1) * 128], iP16[0:8, :], float(e_),
                None, ALU.is_equal, ALU.bypass))
            bank = 4 + tc % 2
            p.wait("t", "v", voh)
            if tc >= 2:
                p.wait("t", "v", sbc_done[tc - 2])
            tm = p.op("t", lambda e, bank=bank, ohb=ohb, tc=tc: e.matmul(
                pb[bank][:, 0:512], onehot[0:8, ohb * 128:(ohb + 1) * 128],
                g("spos0")[0:8, tc * 512:(tc + 1) * 512], start=True, stop=True))
            oh_free_t[ohb] = tm
            p.wait("v", "t", tm)
            sbc_done[tc] = p.op("v", lambda e, bank=bank, tc=tc: e.tensor_copy(
                g("sbc")[:, tc * 512:(tc + 1) * 512], pb[bank][:, 0:512]))
        ptb = None
        for st in range(NST):
            p.wait("v", "t", scatter_t_last)
            if e_ == 0:
                p.wait("v", "t", u_last)
                p.wait("v", "v", v_si[(SC - 1, 3)])
            p.wait("v", "v", prow_done[3])
            ptb = p.op("v", lambda e, st=st: e.scalar_tensor_tensor(
                g("pt")[:, st * T:(st + 1) * T], g("sbc")[:, :], g("iPst")[:, st:st + 1],
                g("prow")[:, :], ALU.is_equal, ALU.mult))
        pt_built[e_] = ptb

        for m in range(4):
            w_sched(e_, m, 2)

        y_stop = {}
        for m in range(MB):
            cstr, cval = cast_val[2][(e_, m)]
            p.wait("t", cstr, cval)
            p.wait("t", "a", relu[(e_, m)])
            for dc in range(DC):
                if m == 0:
                    if dc < 4:
                        p.wait("t", "a", relu[(e_, 28 + dc)])
                    elif dc == 4:
                        p.wait("t", "v", sbc_done[2])
                    elif dc == 5:
                        p.wait("t", "v", sbc_done[3])
                r = p.op("t", lambda e, m=m, dc=dc: e.matmul(
                    pb[dc][:, 0:CAP],
                    g("w2r")[:, (m % 4) * 1024 + dc * 128:(m % 4) * 1024 + (dc + 1) * 128],
                    g("h_sb")[:, m * CAP:(m + 1) * CAP],
                    start=(m == 0), stop=(m == MB - 1)), inc=(m == MB - 1 or dc == DC - 1))
                if dc == DC - 1:
                    ring_free[2][e_ * MB + m] = r
                if m == MB - 1:
                    y_stop[dc] = r
            if m + 4 < MB:
                w_sched(e_, m + 4, 2)
        if e_ + 1 < E:
            for m2 in range(4):
                w_sched(e_ + 1, m2, 1, ylast_t=y_stop[DC - 1])
        tr_done = {}
        a_yt = None
        for dc in range(DC):
            ping = dc % 2
            p.wait("a", "t", y_stop[dc])
            if dc >= 2:
                p.wait("a", "t", tr_done[dc - 2])
            a_yb = p.op("a", lambda e, dc=dc, ping=ping, e_=e_: e.activation(
                g("ybf")[:, ping * CAP:(ping + 1) * CAP], pb[dc][:, 0:CAP], AF.Identity,
                bias=g("b2v")[:, e_ * DC + dc:e_ * DC + dc + 1]))
            tt = None
            for st in range(NST):
                p.wait("t", "a", a_yb)
                p.wait("t", "v", scatter_add_last)
                tt = p.op("t", lambda e, dc=dc, st=st, ping=ping: e.transpose(
                    pb[dc][:, st * 128:(st + 1) * 128],
                    g("ybf")[:, ping * CAP + st * 128:ping * CAP + (st + 1) * 128],
                    id32[:, :]))
            tr_done[dc] = tt
            for st in range(NST):
                p.wait("a", "t", tt)
                if e_ == 0:
                    p.wait("a", "t", c_last_t)
                a_yt = p.op("a", lambda e, dc=dc, st=st: e.activation(
                    g("yt")[:, st * 1024 + dc * 128:st * 1024 + (dc + 1) * 128],
                    pb[dc][:, st * 128:(st + 1) * 128], AF.Identity))
        a_yt_store[e_] = a_yt

        sc_add = {}
        for i in range(NT):
            for dh in range(2):
                idx = i * 2 + dh
                bank = idx % 2
                tsc = None
                for st in range(NST):
                    p.wait("t", "v", pt_built[e_])
                    p.wait("t", "a", a_yt_store[e_])
                    if idx >= 2:
                        p.wait("t", "v", sc_add[idx - 2])
                    r = p.op("t", lambda e, bank=bank, st=st, i=i, dh=dh: e.matmul(
                        pb[bank][:, 0:512],
                        g("pt")[:, st * T + i * 128:st * T + (i + 1) * 128],
                        g("yt")[:, st * 1024 + dh * 512:st * 1024 + (dh + 1) * 512],
                        start=(st == 0), stop=(st == NST - 1)), inc=(st == NST - 1))
                    if r is not None:
                        tsc = r
                p.wait("v", "t", tsc)
                sc_add[idx] = p.op("v", lambda e, bank=bank, i=i, dh=dh: e.tensor_tensor(
                    out_acc[:, i * D + dh * 512:i * D + (dh + 1) * 512],
                    out_acc[:, i * D + dh * 512:i * D + (dh + 1) * 512],
                    pb[bank][:, 0:512], ALU.add))
                scatter_t_last = tsc
                if e_ == E - 1 and dh == 1:
                    p.wait("g", "v", sc_add[idx])
                    p.dma("g", "go", dram["out"][i * 128:(i + 1) * 128, :],
                          out_acc[:, i * D:(i + 1) * D])
                    out_sent += 1
        scatter_add_last = sc_add[NT * 2 - 1]

    assert out_sent == NT
    p.op("g", lambda e, v=p.cnt["go"]: e.wait_ge(p.sems["go"], v), inc=False)


def kernel(**inputs):
    xfull = np.ascontiguousarray(inputs["L0_attn_u"], dtype=np.float32)
    W1 = np.asarray(inputs["L0_ffn_W1"], dtype=np.float32)
    W2 = np.asarray(inputs["L0_ffn_W2"], dtype=np.float32)
    Bm = np.asarray(inputs["L0_attn_Bmat"], dtype=np.float32)
    Cm = np.asarray(inputs["L0_attn_C"], dtype=np.float32)
    wg = np.asarray(inputs["L0_ffn_w_gate"], dtype=np.float32)
    bg = np.asarray(inputs["L0_ffn_b_gate"], dtype=np.float32)
    dt = np.asarray(inputs["L0_attn_dt"], dtype=np.float32)
    Av = np.asarray(inputs["L0_attn_A"], dtype=np.float32)
    Dv = np.asarray(inputs["L0_attn_D"], dtype=np.float32)
    nw = np.asarray(inputs["L0_attn_norm_w"], dtype=np.float32)
    b1 = np.asarray(inputs["L0_ffn_b1"], dtype=np.float32)
    b2 = np.asarray(inputs["L0_ffn_b2"], dtype=np.float32)

    w1h = np.ascontiguousarray(
        W1.reshape(E, DC, 128, MB, 128).transpose(2, 0, 3, 1, 4).reshape(128, -1))
    w2h = np.ascontiguousarray(
        W2.reshape(E, MB, 128, DC, 128).transpose(2, 0, 1, 3, 4).reshape(128, -1))
    bmh = np.ascontiguousarray(
        Bm.reshape(DC, 128, SC, 128).transpose(1, 0, 2, 3).reshape(128, -1))
    cmh = np.ascontiguousarray(
        Cm.reshape(SC, 128, D).transpose(1, 0, 2).reshape(128, SC * D))
    wgh = np.ascontiguousarray(wg.reshape(DC, 128, 8).transpose(1, 0, 2).reshape(128, 64))
    bgh = np.ascontiguousarray(bg.reshape(8, 1))
    dth = np.ascontiguousarray(dt.reshape(4, 128).T)
    avh = np.ascontiguousarray(Av.reshape(4, 128).T)
    dvh = np.ascontiguousarray(Dv.reshape(8, 128).T)
    nwh = np.ascontiguousarray(nw.reshape(8, 128).T)
    b1h = np.ascontiguousarray(b1.reshape(E, MB, 128).transpose(2, 0, 1).reshape(128, 256))
    b2h = np.ascontiguousarray(b2.reshape(E, DC, 128).transpose(2, 0, 1).reshape(128, 64))
    id32 = np.eye(128, dtype=np.float32)
    i384 = np.tile(np.arange(CAP, dtype=np.float32), (128, 1))
    iP16 = np.ascontiguousarray(
        np.broadcast_to(np.arange(16, dtype=np.float32)[:, None], (16, 128)))
    iPst = np.ascontiguousarray(
        (np.arange(128, dtype=np.float32)[:, None]
         + 128.0 * np.arange(NST, dtype=np.float32)[None, :]))
    i8 = np.tile(np.arange(8, dtype=np.float32), (128, 1))
    onec = np.ones((1, 128), dtype=np.float32)

    shared = dict(w1=w1h, w2=w2h, bmat=bmh, cmat=cmh, wg=wgh, bgv=bgh, dtv=dth,
                  av=avh, dvec=dvh, nw=nwh, b1v=b1h, b2v=b2h, id32=id32,
                  i384=i384, iP16=iP16, iPst=iPst, i8=i8, onec=onec)
    in_maps = [dict(x=np.ascontiguousarray(xfull[b]), **shared) for b in range(8)]

    nc = build()
    res = run_bass_kernel_spmd(nc, in_maps, core_ids=list(range(8)))
    return np.stack([res.results[b]["out"] for b in range(8)], axis=0)


if __name__ == "__main__":
    import reference as R
    ins = R.setup_inputs()
    got = kernel(**{k: np.asarray(v) for k, v in ins.items()})
    want = np.asarray(R.reference(**ins))
    err = np.abs(got - want)
    denom = np.mean(want.astype(np.float64) ** 2)
    print("resvar", float(np.mean((got - want).astype(np.float64) ** 2) / denom))
    print("maxabs", float(err.max()))


# version 21
# speedup vs baseline: 1.1167x; 1.1167x over previous
import numpy as np
import ml_dtypes
import concourse.bass as bass
import concourse.mybir as mybir
from concourse.bass_utils import run_bass_kernel_spmd

F32 = mybir.dt.float32
BF16 = mybir.dt.bfloat16
U32 = mybir.dt.uint32
AF = mybir.ActivationFunctionType
ALU = mybir.AluOpType
AX = mybir.AxisListType
BFNP = ml_dtypes.bfloat16

T = 2048
D = 1024
S = 512
E = 8
NT = 16
DC = 8
SC = 4
MB = 32
CAP = 384
NST = 3

ENGS = ("t", "v", "a", "s", "g")


class Prog:
    def __init__(self, sems):
        self.sems = sems
        self.q = {e: [] for e in ENGS}
        self.cnt = {k: 0 for k in sems}
        self.lastwait = {}

    def wait(self, eng, src, val):
        if val is None or val <= 0:
            return
        assert val <= self.cnt[src], f"wait on {src}={val} but only {self.cnt[src]} emitted"
        k = (eng, src)
        if self.lastwait.get(k, 0) >= val:
            return
        self.lastwait[k] = val
        sem = self.sems[src]
        self.q[eng].append(lambda e, sem=sem, v=val: e.wait_ge(sem, v))

    def op(self, eng, fn, inc=True, stream=None):
        if eng in ("v", "a"):
            self.wait(eng, eng, self.cnt[eng])
        if inc:
            key = stream or eng
            self.cnt[key] += 1
            v = self.cnt[key]
            sem = self.sems[key]
            self.q[eng].append(lambda e, fn=fn, sem=sem: fn(e).then_inc(sem, 1))
            return v
        self.q[eng].append(lambda e, fn=fn: fn(e))
        return None

    def dma(self, eng, stream, dst, src, **kw):
        self.cnt[stream] += 16
        v = self.cnt[stream]
        sem = self.sems[stream]
        self.q[eng].append(
            lambda e, dst=dst, src=src, kw=kw, sem=sem: e.dma_start(dst, src, **kw).then_inc(sem, 16))
        return v


def build():
    nc = bass.Bass("TRN2", target_bir_lowering=False)

    dram = {}
    for nm, sh in [("x", [T, D]), ("w1", [E * MB * 128, DC * 128]),
                   ("w2", [E * MB * 128, DC * 128]), ("bmat", [128, DC * SC * 128]),
                   ("cmat", [128, SC * D]), ("wg", [128, 64]), ("bgv", [8, 1]),
                   ("dtv", [128, 4]), ("av", [128, 4]), ("dvec", [128, 8]),
                   ("nw", [128, 8]), ("b1v", [128, 256]), ("b2v", [128, 64]),
                   ("id32", [128, 128]), ("i384", [128, CAP]), ("iP16", [16, 128]),
                   ("iPst", [128, NST]), ("i8", [128, 8]), ("onec", [1, 128])]:
        dram[nm] = nc.dram_tensor(nm, sh, F32, kind="ExternalInput").ap()
    dram["out"] = nc.dram_tensor("out", [T, D], F32, kind="ExternalOutput").ap()
    dram["rowb"] = nc.dram_tensor("rowb_sc", [16, 128], F32).ap()
    dram["rowb2"] = nc.dram_tensor("rowb2_sc", [16, 128], F32).ap()
    dram["rowb3"] = nc.dram_tensor("rowb3_sc", [8, 128], F32).ap()

    SB_BASE = 16512
    LIMIT = 229376
    cur = [SB_BASE]
    A = {}
    OFF = {}

    def at(name, shape, dt, size=None):
        if size is None:
            nb = {F32: 4, BF16: 2, U32: 4}[dt]
            size = int(np.prod(shape[1:])) * nb
        off = cur[0]
        cur[0] += (size + 31) & ~31
        assert cur[0] <= LIMIT, f"SBUF overflow at {name}: {cur[0]}"
        A[name] = nc.alloc_sbuf_tensor_at(name, shape, dt, offset=off)
        OFF[name] = off
        return off

    def over(name, shape, dt, off):
        A[name] = nc.alloc_sbuf_tensor_at(name, shape, dt, offset=off)

    for nm, sh, dt in [
        ("b1v", [128, 256], F32), ("b2v", [128, 64], F32), ("dvec", [128, 8], F32),
        ("nw", [128, 8], F32), ("dtv", [128, 4], F32), ("av", [128, 4], F32),
        ("cst", [128, 2], F32), ("bgv", [8, 1], F32), ("wg", [128, 64], F32),
        ("wgh", [128, 64], BF16), ("wgl", [128, 64], BF16), ("dts", [128, 4], F32),
        ("a_s", [128, 4], F32), ("small", [128, 16], F32), ("mv", [128, 32], F32),
        ("bnbuf", [128, 12], F32), ("statw", [128, 64], F32), ("id32", [128, 128], F32),
        ("iota8", [128, 8], F32), ("i384", [128, CAP], F32), ("iP16", [16, 128], F32),
        ("iPst", [128, NST], F32), ("onehot", [16, 256], F32), ("dvecT", [8, 128], F32),
        ("rstdT", [16, 128], F32), ("p16T", [16, 128], F32), ("mx16", [128, 16], F32),
        ("z16", [128, 16], F32), ("p16f", [128, 16], F32), ("ixf16", [128, 16], F32),
        ("lgall", [128, 128], F32), ("subal", [128, 128], F32), ("maskal", [128, 128], F32),
        ("idxual", [128, 128], U32), ("sposc", [128, 128], F32),
        ("lo2", [128, 256], BF16), ("ybf", [128, 2 * CAP], F32),
        ("onec", [1, 128], F32),
    ]:
        at(nm, sh, dt)
    at("spos0", [8, T], F32)
    at("prow", [128, T], BF16)
    at("xn_tok", [128, NT * D], BF16)
    at("out_acc", [128, NT * D], F32)
    WST = at("wstage", [128, 4 * 1024], F32)
    W1R = at("w1r", [128, 4 * 1024], BF16)
    W2R = at("w2r", [128, 4 * 1024], BF16)
    HSB = at("h_sb", [128, MB * CAP], BF16)
    PRG = at("pring", [128, 2 * CAP], BF16)
    PTO = at("pt", [128, NST * T], BF16)
    XGS = at("xgs", [128, 2048], F32)
    YTO = at("yt", [128, NST * 1024], BF16, size=8192)
    assert cur[0] <= LIMIT, cur[0]

    over("Xg", [128, DC * CAP], BF16, XGS)
    over("sbc", [128, T], F32, XGS)
    over("bmbf", [128, 4096], BF16, WST)
    over("Dbc", [128, D], F32, WST + 8192)
    over("scin", [128, T], F32, WST + 8192)
    over("xnt32_t0", [128, D], F32, WST + 12288)
    over("lg", [8, T], F32, W1R)
    over("xscan0", [128, T], BF16, W1R)
    over("xscan1", [128, T], BF16, W1R + 4096)
    over("mask01", [8, T], F32, W2R)
    over("xscan2", [128, T], BF16, W2R)
    over("xscan3", [128, T], BF16, W2R + 4096)
    over("xnt16", [128, DC * T], BF16, HSB)
    over("xnt32_t1", [128, D], F32, PTO + 6656)
    over("rstd_b", [128, T], F32, XGS)
    over("xn32t0", [128, D], F32, XGS)
    over("xn32t1", [128, D], F32, XGS + 4096)
    over("incl", [8, T], F32, WST + 8192)
    over("cmbf", [128, 4096], BF16, YTO)
    over("lo2f", [128, D], BF16, OFF["ybf"])
    over("w1x", [128, 4 * 1024], BF16, HSB + 16384)

    import contextlib
    with contextlib.ExitStack() as stk:
        pb = [stk.enter_context(nc.psum_tensor(f"pb{i}", [128, 512], F32)) for i in range(8)]
        sem_names = ["t", "v", "a", "sc", "gc", "go", "q0", "q1", "q2"]
        sems = {n: stk.enter_context(nc.semaphore(f"sem_{n}")) for n in sem_names}
        p = Prog(sems)
        emit(p, A, pb, dram)

        with nc.Block() as blk:
            @blk.tensor
            def _(eng):
                for fn in p.q["t"]:
                    fn(eng)

            @blk.vector
            def _(eng):
                for fn in p.q["v"]:
                    fn(eng)

            @blk.scalar
            def _(eng):
                for fn in p.q["a"]:
                    fn(eng)

            @blk.sync
            def _(eng):
                for fn in p.q["s"]:
                    fn(eng)

            @blk.gpsimd
            def _(eng):
                for fn in p.q["g"]:
                    fn(eng)
    return nc


def emit(p, A, pb, dram):
    g = lambda n: A[n]
    out_acc, xn_tok, xnt16 = g("out_acc"), g("xn_tok"), g("xnt16")
    statw, mv, bnbuf, cst, small = g("statw"), g("mv"), g("bnbuf"), g("cst"), g("small")
    id32, onehot, iP16 = g("id32"), g("onehot"), g("iP16")

    for nm_s, nm_d in [("b1v", "b1v"), ("b2v", "b2v"), ("dvec", "dvec"), ("nw", "nw"),
                       ("dtv", "dtv"), ("av", "av"), ("bgv", "bgv"), ("wg", "wg"),
                       ("id32", "id32"), ("i384", "i384"), ("iP16", "iP16"),
                       ("iPst", "iPst"), ("iota8", "i8"), ("onec", "onec")]:
        p.dma("s", "sc", g(nm_s)[:, :], dram[nm_d])
    c_consts = p.cnt["sc"]

    p.op("v", lambda e: e.memset(cst[:, 0:1], 1e-5))
    p.op("v", lambda e: e.memset(cst[:, 1:2], 1e-6))
    v_one = p.op("v", lambda e: e.memset(small[:, 0:1], 1.0))

    p.wait("g", "sc", c_consts)
    swx_bm = p.dma("g", "gc", g("bmbf")[:, :], dram["bmat"])

    p.wait("t", "sc", c_consts)
    t_dT = p.op("t", lambda e: e.transpose(pb[0][0:8, 0:128], g("dvec")[:, :], id32[:, :]))
    p.wait("a", "t", t_dT)
    a_dT = p.op("a", lambda e: e.copy(g("dvecT")[:, :], pb[0][0:8, 0:128]))
    oh_free_t = {0: None, 1: None}
    p.wait("s", "a", a_dT)
    bd1 = p.dma("s", "sc", dram["rowb3"], g("dvecT")[:, :])
    p.wait("s", "sc", bd1)
    bd2 = p.dma("s", "sc", g("Dbc")[0:1, :],
                dram["rowb3"].rearrange("(o a) b -> o (a b)", o=1))
    v_dbc_all = 0
    for hh in range(2):
        bank = hh
        p.wait("t", "sc", bd2)
        tm = p.op("t", lambda e, bank=bank, hh=hh: e.matmul(
            pb[bank][:, 0:512], g("onec")[:, :], g("Dbc")[0:1, hh * 512:(hh + 1) * 512],
            start=True, stop=True))
        p.wait("v", "t", tm)
        v_dbc_all = p.op("v", lambda e, bank=bank, hh=hh: e.tensor_copy(
            g("Dbc")[:, hh * 512:(hh + 1) * 512], pb[bank][:, 0:512]))

    v_wgh = p.op("v", lambda e: e.tensor_copy(g("wgh")[:, :], g("wg")[:, :]))
    v_wgl = p.op("v", lambda e: e.tensor_tensor(g("wgl")[:, :], g("wg")[:, :],
                                                g("wgh")[:, :], ALU.subtract))

    xn32t = [g("xn32t0"), g("xn32t1")]
    xnt32t = [g("xnt32_t0"), g("xnt32_t1")]
    lo2 = g("lo2")
    x_d = dram["x"]
    v_xn, v_tok, a_ev, trv_last, vlo_last = {}, {}, {}, {}, {}
    gate_grp = {}
    a_lg = {}
    tr_i = 0
    for i in range(NT):
        buf = i % 2
        xld = p.dma("s", "sc", out_acc[:, i * D:(i + 1) * D], x_d[i * 128:(i + 1) * 128, :])
        p.wait("v", "sc", xld)
        p.op("v", lambda e, i=i: e.bn_stats(bnbuf[:, 0:6], out_acc[:, i * D:i * D + 512]))
        p.op("v", lambda e, i=i: e.bn_stats(bnbuf[:, 6:12], out_acc[:, i * D + 512:(i + 1) * D]))
        v3 = p.op("v", lambda e, i=i: e.bn_aggr(mv[:, 2 * i:2 * i + 2], bnbuf[:, 0:12]))
        p.wait("a", "v", v3)
        a1 = p.op("a", lambda e, i=i: e.activation(statw[:, 49:50],
                                                   mv[:, 2 * i + 1:2 * i + 2], AF.Ln,
                                                   bias=cst[:, 0:1]))
        a2 = p.op("a", lambda e, i=i: e.activation(statw[:, i:i + 1], statw[:, 49:50],
                                                   AF.Exp, scale=-0.5))
        p.wait("v", "a", a2)
        v5 = p.op("v", lambda e, i=i: e.scalar_tensor_tensor(
            statw[:, 16 + i:17 + i], mv[:, 2 * i:2 * i + 1], -1.0, statw[:, i:i + 1],
            ALU.mult, ALU.mult))
        if i >= 2:
            p.wait("a", "t", trv_last[i - 2])
            p.wait("a", "v", v_tok[i - 2])
        p.wait("a", "v", v5)
        v_xn[i] = p.op("a", lambda e, i=i, b=buf: e.activation(
            xn32t[b][:, :], out_acc[:, i * D:(i + 1) * D], AF.Identity,
            scale=statw[:, i:i + 1], bias=statw[:, 16 + i:17 + i]))
        p.wait("v", "a", v_xn[i])
        v_tok[i] = p.op("v", lambda e, i=i, b=buf: e.tensor_copy(
            xn_tok[:, i * D:(i + 1) * D], xn32t[b][:, :]))
        p.wait("v", "v", v_dbc_all)
        p.op("v", lambda e, i=i: e.tensor_tensor(
            out_acc[:, i * D:(i + 1) * D], out_acc[:, i * D:(i + 1) * D],
            g("Dbc")[:, :], ALU.add))
        p.wait("t", "a", v_xn[i])
        tv = None
        for half in range(2):
            bank = 6 + half
            if i >= 1:
                p.wait("t", "a", a_ev[(i - 1, half)])
            for k4 in range(4):
                dc = half * 4 + k4
                tv = p.op("t", lambda e, bank=bank, dc=dc, k4=k4, b=buf: e.transpose(
                    pb[bank][:, k4 * 128:(k4 + 1) * 128],
                    xn32t[b][:, dc * 128:(dc + 1) * 128], id32[:, :]))
            p.wait("a", "t", tv)
            if i >= 2 and half == 0:
                p.wait("a", "v", vlo_last[i - 2])
            a_ev[(i, half)] = p.op("a", lambda e, bank=bank, half=half, b=buf: e.copy(
                xnt32t[b][:, half * 512:(half + 1) * 512], pb[bank][:, 0:512]))
        trv_last[i] = tv
        p.wait("v", "a", a_ev[(i, 1)])
        dst3 = xnt16.rearrange("p (dc t) -> p dc t", dc=DC)[:, :, i * 128:(i + 1) * 128]
        src3 = xnt32t[buf][:, :].rearrange("p (dc k) -> p dc k", dc=DC)
        vhi = p.op("v", lambda e, dst3=dst3, src3=src3: e.tensor_copy(dst3, src3))
        if i >= 1:
            p.wait("v", "t", gate_grp[(i - 1, DC - 1)])
        lo3 = g("lo2f")[:, :].rearrange("p (dc k) -> p dc k", dc=DC)
        vlo = p.op("v", lambda e, src3=src3, dst3=dst3, lo3=lo3: e.tensor_tensor(
            lo3, src3, dst3, ALU.subtract))
        vlo_last[i] = vlo
        bank = 2 + i // 4
        col = (i % 4) * 128
        first = True
        for dc in range(DC):
            p.wait("t", "v", vhi)
            p.wait("t", "v", vlo)
            p.wait("t", "v", v_wgl)
            last = None
            for pi, (wgt, rhs) in enumerate([
                    (g("wgh"), xnt16[:, dc * T + i * 128:dc * T + (i + 1) * 128]),
                    (g("wgh"), g("lo2f")[:, dc * 128:(dc + 1) * 128]),
                    (g("wgl"), xnt16[:, dc * T + i * 128:dc * T + (i + 1) * 128])]):
                st_ = first
                first = False
                sp_ = (dc == DC - 1 and pi == 2)
                r = p.op("t", lambda e, bank=bank, col=col, wgt=wgt, rhs=rhs, dc=dc,
                         st_=st_, sp_=sp_: e.matmul(
                    pb[bank][0:8, col:col + 128], wgt[:, dc * 8:(dc + 1) * 8], rhs,
                    start=st_, stop=sp_), inc=(pi == 2))
                if r is not None:
                    last = r
            gate_grp[(i, dc)] = last
        if i % 4 == 3:
            p.wait("a", "t", gate_grp[(i, DC - 1)])
            a_lg[i // 4] = p.op("a", lambda e, bank=bank, q=i // 4: e.activation(
                g("lg")[0:8, q * 512:(q + 1) * 512], pb[bank][0:8, 0:512], AF.Identity,
                bias=g("bgv")[:, 0:1]))
    gate_last_t = gate_grp[(NT - 1, DC - 1)]

    a1b = p.op("a", lambda e: e.activation(statw[:, 48:64], mv[:, 1:32:2], AF.Ln, bias=cst[:, 0:1]))
    a2b = p.op("a", lambda e: e.activation(statw[:, 48:64], statw[:, 48:64], AF.Exp, scale=-1.0))
    p.wait("v", "a", a2b)
    v_q = p.op("v", lambda e: e.tensor_tensor(statw[:, 32:48], mv[:, 1:32:2],
                                              statw[:, 48:64], ALU.mult))
    p.wait("a", "v", v_q)
    p.op("a", lambda e: e.activation(statw[:, 48:64], statw[:, 32:48], AF.Ln, bias=cst[:, 1:2]))
    a4b = p.op("a", lambda e: e.activation(statw[:, 32:48], statw[:, 48:64], AF.Exp, scale=-0.5))

    p.wait("t", "a", a4b)
    p.wait("t", "a", a_ev[(NT - 1, 0)])
    tvr = p.op("t", lambda e: e.transpose(pb[6][0:16, 0:128], statw[:, 32:48], id32[:, :]))
    p.wait("a", "t", tvr)
    a_rT = p.op("a", lambda e: e.copy(g("rstdT")[:, :], pb[6][0:16, 0:128]))
    p.wait("s", "a", a_rT)
    br1 = p.dma("s", "sc", dram["rowb"], g("rstdT")[:, :])
    p.wait("s", "sc", br1)
    p.wait("s", "v", vlo_last[NT - 1])
    br2 = p.dma("s", "sc", g("rstd_b")[0:1, :],
                dram["rowb"].rearrange("(o a) b -> o (a b)", o=1))
    rstd_done = {}
    for tc4 in range(4):
        bank = 4 + tc4 % 2
        p.wait("t", "sc", br2)
        p.wait("t", "a", a_lg[2] if tc4 % 2 == 0 else a_lg[3])
        if tc4 >= 2:
            p.wait("t", "v", rstd_done[tc4 - 2])
        tm = p.op("t", lambda e, bank=bank, tc4=tc4: e.matmul(
            pb[bank][:, 0:512], g("onec")[:, :],
            g("rstd_b")[0:1, tc4 * 512:(tc4 + 1) * 512], start=True, stop=True))
        p.wait("v", "t", tm)
        rstd_done[tc4] = p.op("v", lambda e, bank=bank, tc4=tc4: e.tensor_copy(
            g("rstd_b")[:, tc4 * 512:(tc4 + 1) * 512], pb[bank][:, 0:512]))

    tm1_all = None
    p.wait("t", "a", a_rT)
    for i in range(NT):
        p.wait("t", "a", a_lg[i // 4])
        tm1_all = p.op("t", lambda e, i=i: e.transpose(
            pb[6][:, i * 8:(i + 1) * 8], g("lg")[0:8, i * 128:(i + 1) * 128],
            id32[0:8, 0:8]))
    lg3 = lambda ap: ap.rearrange("p (s j) -> p s j", s=16)
    mxB = g("mx16")[:, :].unsqueeze(2).broadcast_to([128, 16, 8])
    p.wait("v", "t", tm1_all)
    v_lgall = p.op("v", lambda e: e.tensor_copy(g("lgall")[:, :], pb[6][:, 0:128]))
    p.op("v", lambda e: e.tensor_reduce(g("mx16")[:, :], lg3(g("lgall")[:, :]), axis=AX.X,
                                        op=ALU.max))
    v_sub = p.op("v", lambda e: e.tensor_tensor(lg3(g("subal")[:, :]), lg3(g("lgall")[:, :]),
                                                mxB, ALU.subtract))
    p.op("v", lambda e: e.tensor_tensor(lg3(g("maskal")[:, :]), mxB, mxB, ALU.max))
    for i in range(NT):
        p.op("v", lambda e, i=i: e.max_index(
            g("idxual")[:, i * 8:(i + 1) * 8], g("maskal")[:, i * 8:(i + 1) * 8],
            g("lgall")[:, i * 8:(i + 1) * 8]))
    v_ixf = p.op("v", lambda e: e.tensor_copy(g("ixf16")[:, :], g("idxual")[:, 0:128:8]))
    p.wait("a", "v", v_ixf)
    aex = p.op("a", lambda e: e.activation(g("lgall")[:, :], g("subal")[:, :], AF.Exp))
    p.wait("v", "a", aex)
    p.op("v", lambda e: e.tensor_reduce(g("z16")[:, :], lg3(g("lgall")[:, :]), axis=AX.X,
                                        op=ALU.add))
    p.op("v", lambda e: e.reciprocal(g("p16f")[:, :], g("z16")[:, :]))
    vmk = None
    for i in range(NT):
        vmk = p.op("v", lambda e, i=i: e.tensor_scalar(
            g("maskal")[:, i * 8:(i + 1) * 8], g("iota8")[:, :], g("ixf16")[:, i:i + 1],
            None, ALU.is_equal, ALU.bypass))

    p.wait("t", "v", rstd_done[2])
    p.wait("t", "v", rstd_done[3])
    p.wait("t", "v", vmk)
    tpT = p.op("t", lambda e: e.transpose(pb[4][0:16, 0:128], g("p16f")[:, :], id32[:, :]))
    p.wait("a", "t", tpT)
    a_pT = p.op("a", lambda e: e.copy(g("p16T")[:, :], pb[4][0:16, 0:128]))
    p.wait("s", "a", a_pT)
    bp1 = p.dma("s", "sc", dram["rowb2"], g("p16T")[:, :])
    p.wait("s", "sc", bp1)
    bp2 = p.dma("s", "sc", g("scin")[0:1, :],
                dram["rowb2"].rearrange("(o a) b -> o (a b)", o=1))
    prow_done = {}
    for tc4 in range(4):
        bank = 4 + tc4 % 2
        p.wait("t", "sc", bp2)
        if tc4 >= 2:
            p.wait("t", "v", prow_done[tc4 - 2])
        else:
            p.wait("t", "v", rstd_done[2 + tc4])
        tm = p.op("t", lambda e, bank=bank, tc4=tc4: e.matmul(
            pb[bank][:, 0:512], g("onec")[:, :],
            g("scin")[0:1, tc4 * 512:(tc4 + 1) * 512], start=True, stop=True))
        p.wait("v", "t", tm)
        prow_done[tc4] = p.op("v", lambda e, bank=bank, tc4=tc4: e.tensor_copy(
            g("prow")[:, tc4 * 512:(tc4 + 1) * 512], pb[bank][:, 0:512]))

    v_m01 = None
    for gq in range(4):
        p.wait("t", "v", vmk)
        if gq >= 1:
            p.wait("t", "v", v_m01)
        tt2 = None
        for k in range(4):
            i = gq * 4 + k
            tt2 = p.op("t", lambda e, i=i, k=k: e.transpose(
                pb[7][0:8, k * 128:(k + 1) * 128], g("maskal")[:, i * 8:(i + 1) * 8],
                id32[:, :]))
        p.wait("v", "t", tt2)
        v_m01 = p.op("v", lambda e, gq=gq: e.tensor_copy(
            g("mask01")[0:8, gq * 512:(gq + 1) * 512], pb[7][0:8, 0:512]))

    p.wait("v", "v", v_one)
    p.op("v", lambda e: e.tensor_tensor_scan(
        g("incl")[0:8, :], small[0:8, 0:1].broadcast_to([8, T]), g("mask01")[0:8, :],
        0.0, ALU.mult, ALU.add))
    p.op("v", lambda e: e.tensor_tensor(g("spos0")[0:8, :], g("incl")[0:8, :],
                                        g("mask01")[0:8, :], ALU.mult))
    v_spos = p.op("v", lambda e: e.tensor_scalar(
        g("spos0")[0:8, :], g("spos0")[0:8, :], 1.0, None, ALU.subtract, ALU.bypass))

    a_sposc = {}
    p.wait("t", "v", v_lgall)
    p.wait("t", "v", v_spos)
    for i in range(NT):
        col = (i % 8) * 8
        if i >= 8:
            p.wait("t", "a", a_sposc[i - 8])
        ts = p.op("t", lambda e, i=i, col=col: e.transpose(
            pb[6][:, col:col + 8], g("spos0")[0:8, i * 128:(i + 1) * 128],
            id32[0:8, 0:8]))
        p.wait("a", "t", ts)
        a_sposc[i] = p.op("a", lambda e, i=i, col=col: e.copy(
            g("sposc")[:, i * 8:(i + 1) * 8], pb[6][:, col:col + 8]))

    v_bm = {}
    for ic in range(DC):
        p.wait("v", "gc", swx_bm)
        v_bm[ic] = p.op("v", lambda e, ic=ic: e.tensor_scalar(
            g("bmbf")[:, ic * 512:(ic + 1) * 512], g("bmbf")[:, ic * 512:(ic + 1) * 512],
            g("nw")[:, ic:ic + 1], None, ALU.mult, ALU.bypass))
    a_dts = p.op("a", lambda e: e.activation(g("dts")[:, :], g("dtv")[:, :], AF.Sigmoid))
    a_ab = p.op("a", lambda e: e.activation(small[:, 4:8], g("av")[:, :], AF.Abs))
    p.wait("v", "a", a_ab)
    v_pr = p.op("v", lambda e: e.tensor_tensor(small[:, 8:12], g("dts")[:, :],
                                               small[:, 4:8], ALU.mult))
    p.wait("a", "v", v_pr)
    a_a = p.op("a", lambda e: e.activation(g("a_s")[:, :], small[:, 8:12], AF.Exp, scale=-1.0))

    p.wait("g", "t", gate_last_t)
    swx_cm = p.dma("g", "gc", g("cmbf")[:, :], dram["cmat"])

    xscan = [g("xscan0"), g("xscan1"), g("xscan2"), g("xscan3")]
    v_scan, v_si = {}, {}
    u_last = 0
    for sc in range(SC):
        p.wait("v", "a", a_a)
        for tc in range(4):
            bank = tc
            tu = None
            for ic in range(DC):
                p.wait("t", "v", v_bm[ic])
                p.wait("t", "v", v_dbc_all)
                p.wait("t", "a", a_lg[0])
                p.wait("t", "a", a_lg[1])
                if sc > 0:
                    p.wait("t", "v", v_si[(sc - 1, tc)])
                r = p.op("t", lambda e, bank=bank, ic=ic, sc=sc, tc=tc: e.matmul(
                    pb[bank][:, 0:512],
                    g("bmbf")[:, (ic * SC + sc) * 128:(ic * SC + sc) * 128 + 128],
                    xnt16[:, ic * T + tc * 512:ic * T + (tc + 1) * 512],
                    start=(ic == 0), stop=(ic == DC - 1)), inc=(ic == DC - 1))
                if r is not None:
                    tu = r
            u_last = tu
            p.wait("v", "t", tu)
            v_si[(sc, tc)] = p.op("v", lambda e, bank=bank, tc=tc, sc=sc: e.scalar_tensor_tensor(
                g("scin")[:, tc * 512:(tc + 1) * 512], pb[bank][:, 0:512],
                g("dts")[:, sc:sc + 1], g("rstd_b")[:, tc * 512:(tc + 1) * 512],
                ALU.mult, ALU.mult))
        p.wait("v", "t", tm1_all)
        v_scan[sc] = p.op("v", lambda e, sc=sc: e.tensor_tensor_scan(
            xscan[sc][:, :], g("a_s")[:, sc:sc + 1].broadcast_to([128, T]),
            g("scin")[:, :], 0.0, ALU.mult, ALU.add))

    c_add = {}
    c_last_t = 0
    for i in range(NT):
        for dh in range(2):
            idx = i * 2 + dh
            bank = 4 + idx % 4
            tcv = None
            for sc in range(SC):
                p.wait("t", "v", v_scan[sc])
                p.wait("t", "gc", swx_cm)
                if idx >= 4:
                    p.wait("t", "v", c_add[idx - 4])
                elif bank == 4:
                    p.wait("t", "v", prow_done[2])
                elif bank == 5:
                    p.wait("t", "v", prow_done[3])
                elif bank == 6:
                    p.wait("t", "a", a_sposc[NT - 1])
                else:
                    p.wait("t", "v", v_m01)
                r = p.op("t", lambda e, bank=bank, sc=sc, i=i, dh=dh: e.matmul(
                    pb[bank][:, 0:512],
                    xscan[sc][:, i * 128:(i + 1) * 128],
                    g("cmbf")[:, sc * 1024 + dh * 512:sc * 1024 + (dh + 1) * 512],
                    start=(sc == 0), stop=(sc == SC - 1)), inc=(sc == SC - 1))
                if r is not None:
                    tcv = r
            c_last_t = tcv
            p.wait("v", "t", tcv)
            c_add[idx] = p.op("v", lambda e, bank=bank, i=i, dh=dh: e.tensor_tensor(
                out_acc[:, i * D + dh * 512:i * D + (dh + 1) * 512],
                out_acc[:, i * D + dh * 512:i * D + (dh + 1) * 512],
                pb[bank][:, 0:512], ALU.add))
    c_add_last = c_add[NT * 2 - 1]

    qmap = ["q0", "q1", "q2"]
    qeng = {"q0": "s", "q1": "a", "q2": "g"}
    stage_cast = []
    stage_k = [0]
    stage_i = [0]
    ring_free = {1: {}, 2: {}}
    cast_val = {1: {}, 2: {}}

    def w_sched(e_, m, which, ylast_t=None):
        k = stage_k[0]
        stage_k[0] += 1
        q = qmap[k % 3]
        eng = qeng[q]
        srcd = dram["w1"] if which == 1 else dram["w2"]
        gidx = e_ * MB + m
        if which == 1 and m < 4:
            dst = g("w1x")[:, m * 1024:(m + 1) * 1024]
            war_t = ylast_t
        else:
            ring = g("w1r") if which == 1 else g("w2r")
            dst = ring[:, (m % 4) * 1024:(m % 4 + 1) * 1024]
            if which == 1:
                wg_ = gidx - 8 if m < 8 else gidx - 4
            else:
                wg_ = gidx - 4
            war_t = ring_free[which].get(wg_) if wg_ >= 0 else None
            if wg_ >= 0:
                assert wg_ in ring_free[which], (which, e_, m)
        if q == "q2":
            p.wait(eng, "t", war_t)
            p.wait(eng, "t", c_last_t)
            v = p.dma(eng, q, dst, srcd[gidx * 128:(gidx + 1) * 128, :])
            cast_val[which][(e_, m)] = (q, v)
        else:
            ks = stage_i[0]
            stage_i[0] += 1
            slot = ks % 4
            if ks >= 4:
                p.wait(eng, "v", stage_cast[ks - 4])
            p.wait(eng, "t", c_last_t)
            v = p.dma(eng, q, g("wstage")[:, slot * 1024:(slot + 1) * 1024],
                      srcd[gidx * 128:(gidx + 1) * 128, :])
            p.wait("v", q, v)
            p.wait("v", "t", war_t)
            cv = p.op("v", lambda e, dst=dst, slot=slot: e.tensor_copy(
                dst, g("wstage")[:, slot * 1024:(slot + 1) * 1024]))
            stage_cast.append(cv)
            cast_val[which][(e_, m)] = ("v", cv)

    gather_t_prev = {}
    scatter_add_last = c_add_last
    scatter_t_last = c_last_t
    pt_built = {}
    a_yt_store = {}
    relu = {}
    out_sent = 0

    for e_ in range(E):
        gather_t = {}
        for i in range(NT):
            pr = i % 2
            p.wait("v", "a", a_sposc[NT - 1])
            war = gather_t[i - 2] if i >= 2 else gather_t_prev.get(14 + i)
            p.wait("v", "t", war)
            if e_ == 0 and i < 2:
                p.wait("v", "t", u_last)
            pbuild = p.op("v", lambda e, i=i, pr=pr, e_=e_: e.tensor_scalar(
                g("pring")[:, pr * CAP:(pr + 1) * CAP], g("i384")[:, :],
                g("sposc")[:, i * 8 + e_:i * 8 + e_ + 1], None, ALU.is_equal, ALU.bypass))
            last = None
            for dc in range(DC):
                p.wait("t", "v", pbuild)
                p.wait("t", "v", v_tok[NT - 1])
                if i == 0:
                    p.wait("t", "v", scatter_add_last)
                r = p.op("t", lambda e, dc=dc, i=i, pr=pr: e.matmul(
                    pb[dc][:, 0:CAP], xn_tok[:, i * D + dc * 128:i * D + (dc + 1) * 128],
                    g("pring")[:, pr * CAP:(pr + 1) * CAP],
                    start=(i == 0), stop=(i == NT - 1)), inc=(dc == DC - 1))
                if r is not None:
                    last = r
            gather_t[i] = last
        gather_t_prev = gather_t
        a_xg = None
        for dc in range(DC):
            p.wait("a", "t", gather_t[NT - 1])
            if e_ > 0:
                p.wait("a", "v", pt_built[e_ - 1])
            a_xg = p.op("a", lambda e, dc=dc: e.activation(
                g("Xg")[:, dc * CAP:(dc + 1) * CAP], pb[dc][:, 0:CAP], AF.Identity))

        if e_ == 0:
            for m in range(4):
                w_sched(e_, m, 1)

        for m in range(MB):
            bank = m % 4
            cstr, cval = cast_val[1][(e_, m)]
            p.wait("t", cstr, cval)
            p.wait("t", "a", a_xg)
            if m >= 4:
                p.wait("t", "a", relu[(e_, m - 4)])
            th = None
            wsrc = g("w1x") if m < 4 else g("w1r")
            wbase = m * 1024 if m < 4 else (m % 4) * 1024
            for dc in range(DC):
                r = p.op("t", lambda e, bank=bank, m=m, dc=dc, wsrc=wsrc, wbase=wbase: e.matmul(
                    pb[bank][:, 0:CAP],
                    wsrc[:, wbase + dc * 128:wbase + (dc + 1) * 128],
                    g("Xg")[:, dc * CAP:(dc + 1) * CAP],
                    start=(dc == 0), stop=(dc == DC - 1)), inc=(dc == DC - 1))
                if r is not None:
                    th = r
            ring_free[1][e_ * MB + m] = th
            p.wait("a", "t", th)
            relu[(e_, m)] = p.op("a", lambda e, bank=bank, m=m, e_=e_: e.activation(
                g("h_sb")[:, m * CAP:(m + 1) * CAP], pb[bank][:, 0:CAP], AF.Relu,
                bias=g("b1v")[:, e_ * MB + m:e_ * MB + m + 1]))
            if m + 4 < MB:
                w_sched(e_, m + 4, 1)
        h_last_t = ring_free[1][e_ * MB + MB - 1]

        sbc_done = {}
        for tc in range(4):
            ohb = tc % 2
            p.wait("v", "t", oh_free_t[ohb])
            p.wait("v", "t", h_last_t)
            voh = p.op("v", lambda e, e_=e_, ohb=ohb: e.tensor_scalar(
                onehot[0:8, ohb * 128:(ohb + 1) * 128], iP16[0:8, :], float(e_),
                None, ALU.is_equal, ALU.bypass))
            bank = 4 + tc % 2
            p.wait("t", "v", voh)
            if tc >= 2:
                p.wait("t", "v", sbc_done[tc - 2])
            tm = p.op("t", lambda e, bank=bank, ohb=ohb, tc=tc: e.matmul(
                pb[bank][:, 0:512], onehot[0:8, ohb * 128:(ohb + 1) * 128],
                g("spos0")[0:8, tc * 512:(tc + 1) * 512], start=True, stop=True))
            oh_free_t[ohb] = tm
            p.wait("v", "t", tm)
            sbc_done[tc] = p.op("v", lambda e, bank=bank, tc=tc: e.tensor_copy(
                g("sbc")[:, tc * 512:(tc + 1) * 512], pb[bank][:, 0:512]))
        ptb = None
        for st in range(NST):
            p.wait("v", "t", scatter_t_last)
            if e_ == 0:
                p.wait("v", "t", u_last)
                p.wait("v", "v", v_si[(SC - 1, 3)])
            p.wait("v", "v", prow_done[3])
            ptb = p.op("v", lambda e, st=st: e.scalar_tensor_tensor(
                g("pt")[:, st * T:(st + 1) * T], g("sbc")[:, :], g("iPst")[:, st:st + 1],
                g("prow")[:, :], ALU.is_equal, ALU.mult))
        pt_built[e_] = ptb

        for m in range(4):
            w_sched(e_, m, 2)

        y_stop = {}
        for m in range(MB):
            cstr, cval = cast_val[2][(e_, m)]
            p.wait("t", cstr, cval)
            p.wait("t", "a", relu[(e_, m)])
            for dc in range(DC):
                if m == 0:
                    if dc < 4:
                        p.wait("t", "a", relu[(e_, 28 + dc)])
                    elif dc == 4:
                        p.wait("t", "v", sbc_done[2])
                    elif dc == 5:
                        p.wait("t", "v", sbc_done[3])
                r = p.op("t", lambda e, m=m, dc=dc: e.matmul(
                    pb[dc][:, 0:CAP],
                    g("w2r")[:, (m % 4) * 1024 + dc * 128:(m % 4) * 1024 + (dc + 1) * 128],
                    g("h_sb")[:, m * CAP:(m + 1) * CAP],
                    start=(m == 0), stop=(m == MB - 1)), inc=(m == MB - 1 or dc == DC - 1))
                if dc == DC - 1:
                    ring_free[2][e_ * MB + m] = r
                if m == MB - 1:
                    y_stop[dc] = r
            if m + 4 < MB:
                w_sched(e_, m + 4, 2)
        if e_ + 1 < E:
            for m2 in range(4):
                w_sched(e_ + 1, m2, 1, ylast_t=y_stop[DC - 1])
        tr_done = {}
        a_yt = None
        for dc in range(DC):
            ping = dc % 2
            p.wait("a", "t", y_stop[dc])
            if dc >= 2:
                p.wait("a", "t", tr_done[dc - 2])
            a_yb = p.op("a", lambda e, dc=dc, ping=ping, e_=e_: e.activation(
                g("ybf")[:, ping * CAP:(ping + 1) * CAP], pb[dc][:, 0:CAP], AF.Identity,
                bias=g("b2v")[:, e_ * DC + dc:e_ * DC + dc + 1]))
            tt = None
            for st in range(NST):
                p.wait("t", "a", a_yb)
                p.wait("t", "v", scatter_add_last)
                tt = p.op("t", lambda e, dc=dc, st=st, ping=ping: e.transpose(
                    pb[dc][:, st * 128:(st + 1) * 128],
                    g("ybf")[:, ping * CAP + st * 128:ping * CAP + (st + 1) * 128],
                    id32[:, :]))
            tr_done[dc] = tt
            for st in range(NST):
                p.wait("a", "t", tt)
                if e_ == 0:
                    p.wait("a", "t", c_last_t)
                a_yt = p.op("a", lambda e, dc=dc, st=st: e.activation(
                    g("yt")[:, st * 1024 + dc * 128:st * 1024 + (dc + 1) * 128],
                    pb[dc][:, st * 128:(st + 1) * 128], AF.Identity))
        a_yt_store[e_] = a_yt

        sc_add = {}
        for i in range(NT):
            for dh in range(2):
                idx = i * 2 + dh
                bank = idx % 2
                tsc = None
                for st in range(NST):
                    p.wait("t", "v", pt_built[e_])
                    p.wait("t", "a", a_yt_store[e_])
                    if idx >= 2:
                        p.wait("t", "v", sc_add[idx - 2])
                    r = p.op("t", lambda e, bank=bank, st=st, i=i, dh=dh: e.matmul(
                        pb[bank][:, 0:512],
                        g("pt")[:, st * T + i * 128:st * T + (i + 1) * 128],
                        g("yt")[:, st * 1024 + dh * 512:st * 1024 + (dh + 1) * 512],
                        start=(st == 0), stop=(st == NST - 1)), inc=(st == NST - 1))
                    if r is not None:
                        tsc = r
                p.wait("v", "t", tsc)
                sc_add[idx] = p.op("v", lambda e, bank=bank, i=i, dh=dh: e.tensor_tensor(
                    out_acc[:, i * D + dh * 512:i * D + (dh + 1) * 512],
                    out_acc[:, i * D + dh * 512:i * D + (dh + 1) * 512],
                    pb[bank][:, 0:512], ALU.add))
                scatter_t_last = tsc
                if e_ == E - 1 and dh == 1:
                    p.wait("g", "v", sc_add[idx])
                    p.dma("g", "go", dram["out"][i * 128:(i + 1) * 128, :],
                          out_acc[:, i * D:(i + 1) * D])
                    out_sent += 1
        scatter_add_last = sc_add[NT * 2 - 1]

    assert out_sent == NT
    p.op("g", lambda e, v=p.cnt["go"]: e.wait_ge(p.sems["go"], v), inc=False)


def kernel(**inputs):
    xfull = np.ascontiguousarray(inputs["L0_attn_u"], dtype=np.float32)
    W1 = np.asarray(inputs["L0_ffn_W1"], dtype=np.float32)
    W2 = np.asarray(inputs["L0_ffn_W2"], dtype=np.float32)
    Bm = np.asarray(inputs["L0_attn_Bmat"], dtype=np.float32)
    Cm = np.asarray(inputs["L0_attn_C"], dtype=np.float32)
    wg = np.asarray(inputs["L0_ffn_w_gate"], dtype=np.float32)
    bg = np.asarray(inputs["L0_ffn_b_gate"], dtype=np.float32)
    dt = np.asarray(inputs["L0_attn_dt"], dtype=np.float32)
    Av = np.asarray(inputs["L0_attn_A"], dtype=np.float32)
    Dv = np.asarray(inputs["L0_attn_D"], dtype=np.float32)
    nw = np.asarray(inputs["L0_attn_norm_w"], dtype=np.float32)
    b1 = np.asarray(inputs["L0_ffn_b1"], dtype=np.float32)
    b2 = np.asarray(inputs["L0_ffn_b2"], dtype=np.float32)

    w1h = np.ascontiguousarray(
        W1.reshape(E, DC, 128, MB, 128).transpose(0, 3, 2, 1, 4).reshape(E * MB * 128, DC * 128))
    w2h = np.ascontiguousarray(
        W2.reshape(E, MB, 128, DC, 128).reshape(E * MB * 128, DC * 128))
    bmh = np.ascontiguousarray(
        Bm.reshape(DC, 128, SC, 128).transpose(1, 0, 2, 3).reshape(128, -1))
    cmh = np.ascontiguousarray(
        Cm.reshape(SC, 128, D).transpose(1, 0, 2).reshape(128, SC * D))
    wgh = np.ascontiguousarray(wg.reshape(DC, 128, 8).transpose(1, 0, 2).reshape(128, 64))
    bgh = np.ascontiguousarray(bg.reshape(8, 1))
    dth = np.ascontiguousarray(dt.reshape(4, 128).T)
    avh = np.ascontiguousarray(Av.reshape(4, 128).T)
    dvh = np.ascontiguousarray(Dv.reshape(8, 128).T)
    nwh = np.ascontiguousarray(nw.reshape(8, 128).T)
    b1h = np.ascontiguousarray(b1.reshape(E, MB, 128).transpose(2, 0, 1).reshape(128, 256))
    b2h = np.ascontiguousarray(b2.reshape(E, DC, 128).transpose(2, 0, 1).reshape(128, 64))
    id32 = np.eye(128, dtype=np.float32)
    i384 = np.tile(np.arange(CAP, dtype=np.float32), (128, 1))
    iP16 = np.ascontiguousarray(
        np.broadcast_to(np.arange(16, dtype=np.float32)[:, None], (16, 128)))
    iPst = np.ascontiguousarray(
        (np.arange(128, dtype=np.float32)[:, None]
         + 128.0 * np.arange(NST, dtype=np.float32)[None, :]))
    i8 = np.tile(np.arange(8, dtype=np.float32), (128, 1))
    onec = np.ones((1, 128), dtype=np.float32)

    shared = dict(w1=w1h, w2=w2h, bmat=bmh, cmat=cmh, wg=wgh, bgv=bgh, dtv=dth,
                  av=avh, dvec=dvh, nw=nwh, b1v=b1h, b2v=b2h, id32=id32,
                  i384=i384, iP16=iP16, iPst=iPst, i8=i8, onec=onec)
    in_maps = [dict(x=np.ascontiguousarray(xfull[b]), **shared) for b in range(8)]

    nc = build()
    res = run_bass_kernel_spmd(nc, in_maps, core_ids=list(range(8)))
    return np.stack([res.results[b]["out"] for b in range(8)], axis=0)


if __name__ == "__main__":
    import reference as R
    ins = R.setup_inputs()
    got = kernel(**{k: np.asarray(v) for k, v in ins.items()})
    want = np.asarray(R.reference(**ins))
    err = np.abs(got - want)
    denom = np.mean(want.astype(np.float64) ** 2)
    print("resvar", float(np.mean((got - want).astype(np.float64) ** 2) / denom))
    print("maxabs", float(err.max()))
